# version 02 (final)
# baseline (speedup 1.0000x reference)
import numpy as np
import ml_dtypes
from contextlib import ExitStack
import concourse.bass as bass
import concourse.mybir as mybir
from concourse.bass_utils import run_bass_kernel_spmd

F32 = mybir.dt.float32
BF16 = mybir.dt.bfloat16
I32 = mybir.dt.int32
AF = mybir.ActivationFunctionType
ALU = mybir.AluOpType
AX = mybir.AxisListType
BFNP = ml_dtypes.bfloat16

NCORE = 8
D = 2048
T = 16384
TC = T // NCORE
S = 4096
NB = 4


class Seq:

    def __init__(self, nc):
        self.nc = nc
        self.t = 0
        self.ops = {k: [] for k in ("sync", "tensor", "vector", "scalar", "gpsimd")}

    def _emit(self, eng, wait, thunk, inc):
        self.ops[eng].append((wait, thunk, inc))

    def dma(self, pairs):
        w = self.t
        first = True
        for (o, i) in pairs:
            self._emit("sync", w if first else None,
                       (lambda e, o=o, i=i: e.dma_start(out=o, in_=i)), 16)
            first = False
        self.t += 16 * len(pairs)

    def mm(self, groups):
        w = self.t
        n = len(groups)
        for idx, (o, l, r, st, sp) in enumerate(groups):
            inc = 1 if idx == n - 1 else 0
            self._emit("tensor", w if idx == 0 else None,
                       (lambda e, o=o, l=l, r=r, st=st, sp=sp:
                        e.matmul(o, l, r, start=st, stop=sp)), inc)
            self.t += inc

    def tr(self, out, in_, ident):
        self._emit("tensor", self.t,
                   (lambda e, o=out, i=in_, d=ident: e.transpose(o, i, d)), 1)
        self.t += 1

    def op(self, eng, thunk):
        self._emit(eng, self.t, thunk, 1)
        self.t += 1

    def build(self, block, sem):
        total = self.t
        engmap = {"sync": block.sync, "tensor": block.tensor,
                  "vector": block.vector, "scalar": block.scalar,
                  "gpsimd": block.gpsimd}
        for name, lst in self.ops.items():
            if not lst and name != "sync":
                continue

            def make(lst, final):
                def body(eng):
                    for (wait, thunk, inc) in lst:
                        if wait is not None and wait > 0:
                            eng.wait_ge(sem, wait)
                        h = thunk(eng)
                        if inc:
                            h.then_inc(sem, inc)
                    if final:
                        eng.wait_ge(sem, total)
                return body
            engmap[name](make(lst, name == "sync"))


def _ln_emit(sq, nc, x_h, t0, ntt, lnxT, W, bufs, identb, psum_trb):
    xsb, xc, lnbf, s1, s2 = bufs
    for tt in range(ntt):
        r0 = t0 + tt * 128
        sq.dma([(xsb[:, :], x_h[r0:r0 + 128, :])])
        sq.op("vector", lambda e: e.reduce_sum(s1[:, :], xsb[:, :], axis=AX.X))
        sq.op("scalar", lambda e: e.mul(s2[:, :], s1[:, :], -1.0 / D))
        sq.op("vector", lambda e: e.tensor_scalar(xc[:, :], xsb[:, :], s2[:, :], None, ALU.add))
        sq.op("scalar", lambda e: e.square(xsb[:, :], xc[:, :]))
        sq.op("vector", lambda e: e.reduce_sum(s1[:, :], xsb[:, :], axis=AX.X))
        sq.op("vector", lambda e: e.tensor_scalar(s2[:, :], s1[:, :], 1.0 / D, 1e-5, ALU.mult, ALU.add))
        sq.op("scalar", lambda e: e.sqrt(s1[:, :], s2[:, :]))
        sq.op("vector", lambda e: e.reciprocal(s2[:, :], s1[:, :]))
        sq.op("vector", lambda e: e.tensor_scalar(lnbf[:, :], xc[:, :], s2[:, :], None, ALU.mult))
        for kt in range(16):
            sq.tr(psum_trb[:, :], lnbf[:, kt * 128:(kt + 1) * 128], identb[:, :])
            c0 = kt * W + tt * 128
            sq.op("scalar", lambda e, c0=c0: e.copy(lnxT[:, c0:c0 + 128], psum_trb[:, :]))


def build_qkv(kind):
    nc = bass.Bass("TRN2", target_bir_lowering=False)
    x_h = nc.dram_tensor("x", [TC, D], F32, kind="ExternalInput")
    if kind == "l0a":
        NQ, NK, rope_q, rope_k, qscale = 2048, 512, True, True, None
    elif kind == "mha":
        NQ, NK, rope_q, rope_k, qscale = 2048, 2048, True, True, None
    else:
        NQ, NK, rope_q, rope_k, qscale = 2048, 512, False, False, None
    wq_h = nc.dram_tensor("wq", [NQ // 128, 128, D], BF16, kind="ExternalInput")
    wk_h = nc.dram_tensor("wk", [NK // 128, 128, D], BF16, kind="ExternalInput")
    wv_h = nc.dram_tensor("wv", [NK // 128, 128, D], BF16, kind="ExternalInput")
    q_o = nc.dram_tensor("q", [NQ, TC], BF16, kind="ExternalOutput")
    k_o = nc.dram_tensor("k", [NK, TC], BF16, kind="ExternalOutput")
    v_o = nc.dram_tensor("v", [NK, TC], BF16, kind="ExternalOutput")
    if rope_q:
        cq_h = nc.dram_tensor("cosq", [128, TC], F32, kind="ExternalInput")
        sq_h = nc.dram_tensor("sinq", [128, TC], F32, kind="ExternalInput")
        ck_h = nc.dram_tensor("cosk", [128, TC], F32, kind="ExternalInput")
        sk_h = nc.dram_tensor("sink", [128, TC], F32, kind="ExternalInput")
    ib_h = nc.dram_tensor("identb", [128, 128], BF16, kind="ExternalInput")

    with ExitStack() as st:
        lnxT = st.enter_context(nc.sbuf_tensor("lnxT", [128, 16 * TC], BF16))
        xsb = st.enter_context(nc.sbuf_tensor("xsb", [128, D], F32))
        xc = st.enter_context(nc.sbuf_tensor("xc", [128, D], F32))
        lnbf = st.enter_context(nc.sbuf_tensor("lnbf", [128, D], BF16))
        s1 = st.enter_context(nc.sbuf_tensor("s1", [128, 1], F32))
        s2 = st.enter_context(nc.sbuf_tensor("s2", [128, 1], F32))
        identb = st.enter_context(nc.sbuf_tensor("identb_s", [128, 128], BF16))
        wbuf = st.enter_context(nc.sbuf_tensor("wbuf", [128, D], BF16))
        t1 = st.enter_context(nc.sbuf_tensor("t1", [128, 512], F32))
        t2 = st.enter_context(nc.sbuf_tensor("t2", [128, 512], F32))
        rot = st.enter_context(nc.sbuf_tensor("rot", [128, 512], F32))
        outb = st.enter_context(nc.sbuf_tensor("outb", [128, 512], BF16))
        if rope_q:
            cqs = st.enter_context(nc.sbuf_tensor("cqs", [128, TC], F32))
            sqs = st.enter_context(nc.sbuf_tensor("sqs", [128, TC], F32))
            cks = st.enter_context(nc.sbuf_tensor("cks", [128, TC], F32))
            sks = st.enter_context(nc.sbuf_tensor("sks", [128, TC], F32))
        psum = st.enter_context(nc.psum_tensor("ps", [128, 512], F32))
        psum_trb = st.enter_context(nc.psum_tensor("ptrb", [128, 128], BF16))
        sem = st.enter_context(nc.semaphore("gs"))

        sq = Seq(nc)
        sq.dma([(identb[:, :], ib_h[:, :])])
        if rope_q:
            sq.dma([(cqs[:, :], cq_h[:, :]), (sqs[:, :], sq_h[:, :]),
                    (cks[:, :], ck_h[:, :]), (sks[:, :], sk_h[:, :])])
        _ln_emit(sq, nc, x_h, 0, 16, lnxT, TC, (xsb, xc, lnbf, s1, s2), identb, psum_trb)

        def linear(w_h, N, out_h, rope, ctab, stab, scl):
            for nt in range(N // 128):
                sq.dma([(wbuf[:, :], w_h[nt, :, :])])
                for tc in range(TC // 512):
                    c0 = tc * 512
                    g = []
                    for kt in range(16):
                        g.append((psum[:, :], wbuf[:, kt * 128:(kt + 1) * 128],
                                  lnxT[:, kt * TC + c0:kt * TC + c0 + 512],
                                  kt == 0, kt == 15))
                    sq.mm(g)
                    if rope:
                        sq.op("vector", lambda e, c0=c0, ct=ctab: e.tensor_mul(t1[:, :], psum[:, :], ct[:, c0:c0 + 512]))
                        sq.op("scalar", lambda e: e.mul(rot[0:64, :], psum[64:128, :], -1.0))
                        sq.op("scalar", lambda e: e.copy(rot[64:128, :], psum[0:64, :]))
                        sq.op("vector", lambda e, c0=c0, stb=stab: e.tensor_mul(t2[:, :], rot[:, :], stb[:, c0:c0 + 512]))
                        sq.op("vector", lambda e: e.tensor_add(outb[:, :], t1[:, :], t2[:, :]))
                    else:
                        s = 1.0 if scl is None else scl
                        sq.op("scalar", lambda e, s=s: e.activation(outb[:, :], psum[:, :], AF.Copy, bias=0.0, scale=s))
                    sq.dma([(out_h[nt * 128:(nt + 1) * 128, c0:c0 + 512], outb[:, :])])

        linear(wq_h, NQ, q_o, rope_q, cqs if rope_q else None, sqs if rope_q else None, qscale)
        linear(wk_h, NK, k_o, rope_k, cks if rope_q else None, sks if rope_q else None, None)
        linear(wv_h, NK, v_o, False, None, None, None)

        with nc.Block() as block:
            sq.build(block, sem)
    return nc


def build_wo_res():
    nc = bass.Bass("TRN2", target_bir_lowering=False)
    a_h = nc.dram_tensor("a", [D, TC], BF16, kind="ExternalInput")
    w_h = nc.dram_tensor("w", [16, 128, D], BF16, kind="ExternalInput")
    x_h = nc.dram_tensor("x", [TC, D], F32, kind="ExternalInput")
    if_h = nc.dram_tensor("identf", [128, 128], F32, kind="ExternalInput")
    xo_h = nc.dram_tensor("xo", [TC, D], F32, kind="ExternalOutput")
    with ExitStack() as st:
        asb = st.enter_context(nc.sbuf_tensor("asb", [128, 16 * TC], BF16))
        wbuf = st.enter_context(nc.sbuf_tensor("wbuf", [128, D], BF16))
        identf = st.enter_context(nc.sbuf_tensor("identf_s", [128, 128], F32))
        ysb = st.enter_context(nc.sbuf_tensor("ysb", [128, 512], F32))
        xsub = st.enter_context(nc.sbuf_tensor("xsub", [128, 128], F32))
        osb = st.enter_context(nc.sbuf_tensor("osb", [128, 128], F32))
        psum = st.enter_context(nc.psum_tensor("ps", [128, 512], F32))
        ptrf = st.enter_context(nc.psum_tensor("ptrf", [128, 128], F32))
        sem = st.enter_context(nc.semaphore("gs"))
        sq = Seq(nc)
        sq.dma([(identf[:, :], if_h[:, :])])
        sq.dma([(asb[:, kt * TC:(kt + 1) * TC], a_h[kt * 128:(kt + 1) * 128, :])
                for kt in range(16)])
        for nt in range(16):
            sq.dma([(wbuf[:, :], w_h[nt, :, :])])
            for tc in range(TC // 512):
                c0 = tc * 512
                g = []
                for kt in range(16):
                    g.append((psum[:, :], wbuf[:, kt * 128:(kt + 1) * 128],
                              asb[:, kt * TC + c0:kt * TC + c0 + 512], kt == 0, kt == 15))
                sq.mm(g)
                sq.op("scalar", lambda e: e.copy(ysb[:, :], psum[:, :]))
                for i in range(4):
                    r0 = c0 + i * 128
                    sq.tr(ptrf[:, :], ysb[:, i * 128:(i + 1) * 128], identf[:, :])
                    sq.dma([(xsub[:, :], x_h[r0:r0 + 128, nt * 128:(nt + 1) * 128])])
                    sq.op("vector", lambda e: e.tensor_add(osb[:, :], ptrf[:, :], xsub[:, :]))
                    sq.dma([(xo_h[r0:r0 + 128, nt * 128:(nt + 1) * 128], osb[:, :])])
        with nc.Block() as block:
            sq.build(block, sem)
    return nc


def build_lightning():
    nc = bass.Bass("TRN2", target_bir_lowering=False)
    q_h = nc.dram_tensor("q", [8 * 128, S], BF16, kind="ExternalInput")
    kt_h = nc.dram_tensor("kT", [8 * 128, S], BF16, kind="ExternalInput")
    kn_h = nc.dram_tensor("kn", [8 * S, 128], BF16, kind="ExternalInput")
    vn_h = nc.dram_tensor("vn", [8 * S, 128], BF16, kind="ExternalInput")
    dg_h = nc.dram_tensor("diagT", [8 * 64, 64], F32, kind="ExternalInput")
    qd_h = nc.dram_tensor("qdecB", [8 * 128, 64], F32, kind="ExternalInput")
    kd_h = nc.dram_tensor("kdec", [8 * 64, 1], F32, kind="ExternalInput")
    bk_h = nc.dram_tensor("blkB", [8 * 128, 1], F32, kind="ExternalInput")
    o_h = nc.dram_tensor("o", [8 * 128, S], BF16, kind="ExternalOutput")
    NCH = S // 64
    with ExitStack() as st:
        qsb = st.enter_context(nc.sbuf_tensor("qsb", [128, S], BF16))
        ktb = st.enter_context(nc.sbuf_tensor("ktb", [128, S], BF16))
        knb = st.enter_context(nc.sbuf_tensor("knb", [64, NCH * 128], BF16))
        vnb = st.enter_context(nc.sbuf_tensor("vnb", [64, NCH * 128], BF16))
        dgb = st.enter_context(nc.sbuf_tensor("dgb", [64, 64], F32))
        qdb = st.enter_context(nc.sbuf_tensor("qdb", [128, 64], F32))
        kdb = st.enter_context(nc.sbuf_tensor("kdb", [64, 1], F32))
        bkb = st.enter_context(nc.sbuf_tensor("bkb", [128, 1], F32))
        kvf = st.enter_context(nc.sbuf_tensor("kvf", [128, 128], F32))
        kvb = st.enter_context(nc.sbuf_tensor("kvb", [128, 128], BF16))
        kvt = st.enter_context(nc.sbuf_tensor("kvt", [128, 128], F32))
        smb = st.enter_context(nc.sbuf_tensor("smb", [64, 64], BF16))
        qsc = st.enter_context(nc.sbuf_tensor("qsc", [128, 64], BF16))
        ksc = st.enter_context(nc.sbuf_tensor("ksc", [64, 128], BF16))
        obuf = st.enter_context(nc.sbuf_tensor("obuf", [128, S], BF16))
        ps_s = st.enter_context(nc.psum_tensor("ps_s", [64, 64], F32))
        ps_o = st.enter_context(nc.psum_tensor("ps_o", [128, 64], F32))
        ps_kv = st.enter_context(nc.psum_tensor("ps_kv", [128, 128], F32))
        sem = st.enter_context(nc.semaphore("gs"))
        sq = Seq(nc)
        for p in range(8):
            sq.dma([(qsb[:, :], q_h[p * 128:(p + 1) * 128, :]),
                    (ktb[:, :], kt_h[p * 128:(p + 1) * 128, :]),
                    (dgb[:, :], dg_h[p * 64:(p + 1) * 64, :]),
                    (qdb[:, :], qd_h[p * 128:(p + 1) * 128, :]),
                    (kdb[:, :], kd_h[p * 64:(p + 1) * 64, :]),
                    (bkb[:, :], bk_h[p * 128:(p + 1) * 128, :])])
            sq.dma([(knb[:, c * 128:(c + 1) * 128], kn_h[p * S + c * 64:p * S + (c + 1) * 64, :])
                    for c in range(NCH)])
            sq.dma([(vnb[:, c * 128:(c + 1) * 128], vn_h[p * S + c * 64:p * S + (c + 1) * 64, :])
                    for c in range(NCH)])
            sq.op("gpsimd", lambda e: e.memset(kvf[:, :], 0.0))
            sq.op("gpsimd", lambda e: e.memset(kvb[:, :], 0.0))
            for c in range(NCH):
                c0 = c * 64
                sq.mm([(ps_s[:, :], ktb[:, c0:c0 + 64], qsb[:, c0:c0 + 64], True, True)])
                sq.op("vector", lambda e: e.tensor_mul(smb[:, :], ps_s[:, :], dgb[:, :]))
                sq.op("vector", lambda e, c0=c0: e.tensor_mul(qsc[:, :], qsb[:, c0:c0 + 64], qdb[:, :]))
                sq.mm([(ps_o[:, :], vnb[:, c * 128:(c + 1) * 128], smb[:, :], True, False),
                       (ps_o[:, :], kvb[:, :], qsc[:, :], False, True)])
                sq.op("scalar", lambda e, c0=c0: e.copy(obuf[:, c0:c0 + 64], ps_o[:, :]))
                sq.op("vector", lambda e, c=c: e.tensor_scalar(ksc[:, :], knb[:, c * 128:(c + 1) * 128], kdb[:, :], None, ALU.mult))
                sq.mm([(ps_kv[:, :], ksc[:, :], vnb[:, c * 128:(c + 1) * 128], True, True)])
                sq.op("vector", lambda e: e.tensor_scalar(kvt[:, :], kvf[:, :], bkb[:, :], None, ALU.mult))
                sq.op("vector", lambda e: e.tensor_add(kvf[:, :], kvt[:, :], ps_kv[:, :]))
                sq.op("vector", lambda e: e.tensor_copy(kvb[:, :], kvf[:, :]))
            sq.dma([(o_h[p * 128:(p + 1) * 128, :], obuf[:, :])])
        with nc.Block() as block:
            sq.build(block, sem)
    return nc


def build_flash():
    nc = bass.Bass("TRN2", target_bir_lowering=False)
    q_h = nc.dram_tensor("q", [8 * 128, S], BF16, kind="ExternalInput")
    kt_h = nc.dram_tensor("kT", [8 * 128, S], BF16, kind="ExternalInput")
    vn_h = nc.dram_tensor("vn", [8 * S, 128], BF16, kind="ExternalInput")
    mk_h = nc.dram_tensor("masks", [4 * 128, 512], F32, kind="ExternalInput")
    ib_h = nc.dram_tensor("identb", [128, 128], BF16, kind="ExternalInput")
    o_h = nc.dram_tensor("o", [8 * 128, S], BF16, kind="ExternalOutput")
    with ExitStack() as st:
        qsb = st.enter_context(nc.sbuf_tensor("qsb", [128, S], BF16))
        ktb = st.enter_context(nc.sbuf_tensor("ktb", [128, S], BF16))
        vsb = st.enter_context(nc.sbuf_tensor("vsb", [128, 32 * 128], BF16))
        sc = st.enter_context(nc.sbuf_tensor("sc", [128, S], F32))
        pb = st.enter_context(nc.sbuf_tensor("pb", [128, S], BF16))
        pb2 = st.enter_context(nc.sbuf_tensor("pb2", [128, S], BF16))
        obuf = st.enter_context(nc.sbuf_tensor("obuf", [128, S], BF16))
        mks = st.enter_context(nc.sbuf_tensor("mks", [128, 4 * 512], F32))
        identb = st.enter_context(nc.sbuf_tensor("identb_s", [128, 128], BF16))
        ptall = st.enter_context(nc.sbuf_tensor("ptall", [128, 32 * 128], BF16))
        mx = st.enter_context(nc.sbuf_tensor("mx", [128, 1], F32))
        nmx = st.enter_context(nc.sbuf_tensor("nmx", [128, 1], F32))
        lsum = st.enter_context(nc.sbuf_tensor("lsum", [128, 1], F32))
        rin = st.enter_context(nc.sbuf_tensor("rin", [128, 1], F32))
        ps_s = st.enter_context(nc.psum_tensor("ps_s", [128, 512], F32))
        ps_tr = st.enter_context(nc.psum_tensor("ps_tr", [128, 128], BF16))
        ps_c = st.enter_context(nc.psum_tensor("ps_c", [128, 128], F32))
        sem = st.enter_context(nc.semaphore("gs"))
        sq = Seq(nc)
        sq.dma([(identb[:, :], ib_h[:, :])])
        sq.dma([(mks[:, r * 512:(r + 1) * 512], mk_h[r * 128:(r + 1) * 128, :]) for r in range(4)])
        for p in range(8):
            sq.dma([(qsb[:, :], q_h[p * 128:(p + 1) * 128, :]),
                    (ktb[:, :], kt_h[p * 128:(p + 1) * 128, :])])
            sq.dma([(vsb[:, jt * 128:(jt + 1) * 128], vn_h[p * S + jt * 128:p * S + (jt + 1) * 128, :])
                    for jt in range(32)])
            for qb in range(32):
                nch = qb // 4 + 1
                W = nch * 512
                r = qb % 4
                for jc in range(nch):
                    sq.mm([(ps_s[:, :], qsb[:, qb * 128:(qb + 1) * 128],
                            ktb[:, jc * 512:(jc + 1) * 512], True, True)])
                    if jc == nch - 1:
                        sq.op("vector", lambda e, jc=jc, r=r: e.tensor_add(
                            sc[:, jc * 512:(jc + 1) * 512], ps_s[:, :], mks[:, r * 512:(r + 1) * 512]))
                    else:
                        sq.op("scalar", lambda e, jc=jc: e.copy(sc[:, jc * 512:(jc + 1) * 512], ps_s[:, :]))
                sq.op("vector", lambda e, W=W: e.reduce_max(mx[:, :], sc[:, 0:W], axis=AX.X))
                sq.op("scalar", lambda e: e.mul(nmx[:, :], mx[:, :], -1.0))
                sq.op("scalar", lambda e, W=W: e.activation(pb[:, 0:W], sc[:, 0:W], AF.Exp, bias=nmx[:, :], scale=1.0))
                sq.op("vector", lambda e, W=W: e.reduce_sum(lsum[:, :], pb[:, 0:W], axis=AX.X))
                sq.op("vector", lambda e: e.reciprocal(rin[:, :], lsum[:, :]))
                sq.op("vector", lambda e, W=W: e.tensor_scalar(pb2[:, 0:W], pb[:, 0:W], rin[:, :], None, ALU.mult))
                njt = W // 128
                for jt in range(njt):
                    sq.tr(ps_tr[:, :], pb2[:, jt * 128:(jt + 1) * 128], identb[:, :])
                    sq.op("scalar", lambda e, jt=jt: e.copy(ptall[:, jt * 128:(jt + 1) * 128], ps_tr[:, :]))
                sq.mm([(ps_c[:, :], vsb[:, jt * 128:(jt + 1) * 128],
                        ptall[:, jt * 128:(jt + 1) * 128], jt == 0, jt == njt - 1)
                       for jt in range(njt)])
                sq.op("scalar", lambda e, qb=qb: e.copy(obuf[:, qb * 128:(qb + 1) * 128], ps_c[:, :]))
            sq.dma([(o_h[p * 128:(p + 1) * 128, :], obuf[:, :])])
        with nc.Block() as block:
            sq.build(block, sem)
    return nc


def build_l0ffn():
    nc = bass.Bass("TRN2", target_bir_lowering=False)
    x_h = nc.dram_tensor("x", [TC, D], F32, kind="ExternalInput")
    wg_h = nc.dram_tensor("wg", [128, 128, D], BF16, kind="ExternalInput")
    wu_h = nc.dram_tensor("wu", [128, 128, D], BF16, kind="ExternalInput")
    wd_h = nc.dram_tensor("wd", [16, 128, 16384], BF16, kind="ExternalInput")
    ib_h = nc.dram_tensor("identb", [128, 128], BF16, kind="ExternalInput")
    if_h = nc.dram_tensor("identf", [128, 128], F32, kind="ExternalInput")
    xo_h = nc.dram_tensor("xo", [TC, D], F32, kind="ExternalOutput")
    with ExitStack() as st:
        lnxT = st.enter_context(nc.sbuf_tensor("lnxT", [128, 16 * 512], BF16))
        inter = st.enter_context(nc.sbuf_tensor("inter", [128, 64 * 512], BF16))
        acc = st.enter_context(nc.sbuf_tensor("acc", [128, 16 * 512], F32))
        wdq = st.enter_context(nc.sbuf_tensor("wdq", [128, 8192], BF16))
        wgb = st.enter_context(nc.sbuf_tensor("wgb", [128, D], BF16))
        wub = st.enter_context(nc.sbuf_tensor("wub", [128, D], BF16))
        xsb = st.enter_context(nc.sbuf_tensor("xsb", [128, D], F32))
        xc = st.enter_context(nc.sbuf_tensor("xc", [128, D], F32))
        lnbf = st.enter_context(nc.sbuf_tensor("lnbf", [128, D], BF16))
        s1 = st.enter_context(nc.sbuf_tensor("s1", [128, 1], F32))
        s2 = st.enter_context(nc.sbuf_tensor("s2", [128, 1], F32))
        ga = st.enter_context(nc.sbuf_tensor("ga", [128, 512], F32))
        gb = st.enter_context(nc.sbuf_tensor("gb", [128, 512], F32))
        gc = st.enter_context(nc.sbuf_tensor("gc", [128, 512], F32))
        ydb = st.enter_context(nc.sbuf_tensor("ydb", [128, 512], F32))
        xsub = st.enter_context(nc.sbuf_tensor("xsub", [128, 128], F32))
        osb = st.enter_context(nc.sbuf_tensor("osb", [128, 128], F32))
        identb = st.enter_context(nc.sbuf_tensor("identb_s", [128, 128], BF16))
        identf = st.enter_context(nc.sbuf_tensor("identf_s", [128, 128], F32))
        ps_g = st.enter_context(nc.psum_tensor("ps_g", [128, 512], F32))
        ps_u = st.enter_context(nc.psum_tensor("ps_u", [128, 512], F32))
        ps_d = st.enter_context(nc.psum_tensor("ps_d", [128, 512], F32))
        ptrb = st.enter_context(nc.psum_tensor("ptrb", [128, 128], BF16))
        ptrf = st.enter_context(nc.psum_tensor("ptrf", [128, 128], F32))
        sem = st.enter_context(nc.semaphore("gs"))
        sq = Seq(nc)
        sq.dma([(identb[:, :], ib_h[:, :]), (identf[:, :], if_h[:, :])])
        for tc in range(4):
            t0 = tc * 512
            _ln_emit(sq, nc, x_h, t0, 4, lnxT, 512, (xsb, xc, lnbf, s1, s2), identb, ptrb)
            for half in range(2):
                for ntl in range(64):
                    nt = half * 64 + ntl
                    sq.dma([(wgb[:, :], wg_h[nt, :, :]), (wub[:, :], wu_h[nt, :, :])])
                    gg = [(ps_g[:, :], wgb[:, kt * 128:(kt + 1) * 128],
                           lnxT[:, kt * 512:(kt + 1) * 512], kt == 0, kt == 15) for kt in range(16)]
                    sq.mm(gg)
                    gu = [(ps_u[:, :], wub[:, kt * 128:(kt + 1) * 128],
                           lnxT[:, kt * 512:(kt + 1) * 512], kt == 0, kt == 15) for kt in range(16)]
                    sq.mm(gu)
                    sq.op("scalar", lambda e: e.square(ga[:, :], ps_g[:, :]))
                    sq.op("vector", lambda e: e.tensor_scalar(gb[:, :], ga[:, :], 0.03567740814, 0.7978845608028654, ALU.mult, ALU.add))
                    sq.op("vector", lambda e: e.tensor_mul(gc[:, :], gb[:, :], ps_g[:, :]))
                    sq.op("scalar", lambda e: e.activation(ga[:, :], gc[:, :], AF.Tanh))
                    sq.op("vector", lambda e: e.tensor_scalar(gb[:, :], ga[:, :], 0.5, 0.5, ALU.mult, ALU.add))
                    sq.op("vector", lambda e: e.tensor_mul(gc[:, :], gb[:, :], ps_g[:, :]))
                    sq.op("vector", lambda e, ntl=ntl: e.tensor_mul(inter[:, ntl * 512:(ntl + 1) * 512], gc[:, :], ps_u[:, :]))
                for ht in range(16):
                    sq.dma([(wdq[:, :], wd_h[ht, :, half * 8192:(half + 1) * 8192])])
                    gd = [(ps_d[:, :], wdq[:, ntl * 128:(ntl + 1) * 128],
                           inter[:, ntl * 512:(ntl + 1) * 512], ntl == 0, ntl == 63) for ntl in range(64)]
                    sq.mm(gd)
                    if half == 0:
                        sq.op("scalar", lambda e, ht=ht: e.copy(acc[:, ht * 512:(ht + 1) * 512], ps_d[:, :]))
                    else:
                        sq.op("vector", lambda e, ht=ht: e.tensor_add(acc[:, ht * 512:(ht + 1) * 512], acc[:, ht * 512:(ht + 1) * 512], ps_d[:, :]))
            for ht in range(16):
                sq.op("scalar", lambda e, ht=ht: e.copy(ydb[:, :], acc[:, ht * 512:(ht + 1) * 512]))
                for i in range(4):
                    r0 = t0 + i * 128
                    sq.tr(ptrf[:, :], ydb[:, i * 128:(i + 1) * 128], identf[:, :])
                    sq.dma([(xsub[:, :], x_h[r0:r0 + 128, ht * 128:(ht + 1) * 128])])
                    sq.op("vector", lambda e: e.tensor_add(osb[:, :], ptrf[:, :], xsub[:, :]))
                    sq.dma([(xo_h[r0:r0 + 128, ht * 128:(ht + 1) * 128], osb[:, :])])
        with nc.Block() as block:
            sq.build(block, sem)
    return nc


def build_smallattn():
    nc = bass.Bass("TRN2", target_bir_lowering=False)
    q_h = nc.dram_tensor("q", [2048, 2048], BF16, kind="ExternalInput")
    k_h = nc.dram_tensor("k", [512, 2048], BF16, kind="ExternalInput")
    v_h = nc.dram_tensor("v", [512, 2048], BF16, kind="ExternalInput")
    o1_h = nc.dram_tensor("onesT", [128, 1], F32, kind="ExternalInput")
    o2_h = nc.dram_tensor("ones1", [1, 128], F32, kind="ExternalInput")
    o_h = nc.dram_tensor("o", [2048, 2048], BF16, kind="ExternalOutput")
    pairs = [(s, sp) for s in range(4) for sp in range(s + 1)]
    pidx = {p: i for i, p in enumerate(pairs)}
    with ExitStack() as st:
        qsb = st.enter_context(nc.sbuf_tensor("qsb", [128, 2048], BF16))
        ksb = st.enter_context(nc.sbuf_tensor("ksb", [128, 2048], BF16))
        vsb = st.enter_context(nc.sbuf_tensor("vsb", [128, 2048], BF16))
        onesT = st.enter_context(nc.sbuf_tensor("onesT_s", [128, 1], F32))
        ones1 = st.enter_context(nc.sbuf_tensor("ones1_s", [1, 128], F32))
        prod = st.enter_context(nc.sbuf_tensor("prod", [128, 512], F32))
        Ssb = st.enter_context(nc.sbuf_tensor("Ssb", [1, 10 * 512], F32))
        Esb = st.enter_context(nc.sbuf_tensor("Esb", [1, 4 * 512], F32))
        Msb = st.enter_context(nc.sbuf_tensor("Msb", [1, 512], F32))
        Lsb = st.enter_context(nc.sbuf_tensor("Lsb", [1, 512], F32))
        Rsb = st.enter_context(nc.sbuf_tensor("Rsb", [1, 512], F32))
        tmp = st.enter_context(nc.sbuf_tensor("tmp", [1, 512], F32))
        accs = st.enter_context(nc.sbuf_tensor("accs", [128, 512], F32))
        trm = st.enter_context(nc.sbuf_tensor("trm", [128, 512], F32))
        outb = st.enter_context(nc.sbuf_tensor("outb", [128, 512], BF16))
        ps_r = st.enter_context(nc.psum_tensor("ps_r", [1, 512], F32))
        ps_b = st.enter_context(nc.psum_tensor("ps_b", [128, 512], F32))
        sem = st.enter_context(nc.semaphore("gs"))
        sq = Seq(nc)
        sq.dma([(onesT[:, :], o1_h[:, :]), (ones1[:, :], o2_h[:, :])])
        SC = 0.08838834764831845
        for h in range(16):
            g = h // 4
            sq.dma([(qsb[:, :], q_h[h * 128:(h + 1) * 128, :]),
                    (ksb[:, :], k_h[g * 128:(g + 1) * 128, :]),
                    (vsb[:, :], v_h[g * 128:(g + 1) * 128, :])])
            for (s, sp) in pairs:
                i = pidx[(s, sp)]
                sq.op("vector", lambda e, s=s, sp=sp: e.tensor_mul(prod[:, :], qsb[:, s * 512:(s + 1) * 512], ksb[:, sp * 512:(sp + 1) * 512]))
                sq.mm([(ps_r[:, :], onesT[:, :], prod[:, :], True, True)])
                sq.op("scalar", lambda e, i=i: e.activation(Ssb[:, i * 512:(i + 1) * 512], ps_r[:, :], AF.Copy, bias=0.0, scale=SC))
            for s in range(4):
                if s == 0:
                    sq.op("vector", lambda e: e.tensor_copy(outb[:, :], vsb[:, 0:512]))
                    sq.dma([(o_h[h * 128:(h + 1) * 128, 0:512], outb[:, :])])
                    continue
                idxs = [pidx[(s, sp)] for sp in range(s + 1)]
                sq.op("scalar", lambda e, i0=idxs[0]: e.copy(Msb[:, :], Ssb[:, i0 * 512:(i0 + 1) * 512]))
                for i in idxs[1:]:
                    sq.op("vector", lambda e, i=i: e.tensor_max(Msb[:, :], Msb[:, :], Ssb[:, i * 512:(i + 1) * 512]))
                for j, i in enumerate(idxs):
                    sq.op("vector", lambda e, i=i: e.tensor_sub(tmp[:, :], Ssb[:, i * 512:(i + 1) * 512], Msb[:, :]))
                    sq.op("scalar", lambda e, j=j: e.activation(Esb[:, j * 512:(j + 1) * 512], tmp[:, :], AF.Exp))
                sq.op("scalar", lambda e: e.copy(Lsb[:, :], Esb[:, 0:512]))
                for j in range(1, s + 1):
                    sq.op("vector", lambda e, j=j: e.tensor_add(Lsb[:, :], Lsb[:, :], Esb[:, j * 512:(j + 1) * 512]))
                sq.op("vector", lambda e: e.reciprocal(Rsb[:, :], Lsb[:, :]))
                for j in range(s + 1):
                    sq.op("vector", lambda e, j=j: e.tensor_mul(Esb[:, j * 512:(j + 1) * 512], Esb[:, j * 512:(j + 1) * 512], Rsb[:, :]))
                for j in range(s + 1):
                    sq.mm([(ps_b[:, :], ones1[:, :], Esb[:, j * 512:(j + 1) * 512], True, True)])
                    sq.op("vector", lambda e, j=j: e.tensor_mul(trm[:, :], ps_b[:, :], vsb[:, j * 512:(j + 1) * 512]))
                    if j == 0:
                        sq.op("vector", lambda e: e.tensor_copy(accs[:, :], trm[:, :]))
                    else:
                        sq.op("vector", lambda e: e.tensor_add(accs[:, :], accs[:, :], trm[:, :]))
                sq.op("vector", lambda e: e.tensor_copy(outb[:, :], accs[:, :]))
                sq.dma([(o_h[h * 128:(h + 1) * 128, s * 512:(s + 1) * 512], outb[:, :])])
        with nc.Block() as block:
            sq.build(block, sem)
    return nc


def build_l1ffn():
    nc = bass.Bass("TRN2", target_bir_lowering=False)
    x_h = nc.dram_tensor("x", [TC, D], F32, kind="ExternalInput")
    wu_h = nc.dram_tensor("wu", [64, 128, D], BF16, kind="ExternalInput")
    wd_h = nc.dram_tensor("wd", [16, 128, 8192], BF16, kind="ExternalInput")
    we_h = nc.dram_tensor("we", [16, 128, D], BF16, kind="ExternalInput")
    gw_h = nc.dram_tensor("gw", [128, 128], BF16, kind="ExternalInput")
    e8_h = nc.dram_tensor("eye8", [8, 8], F32, kind="ExternalInput")
    o2_h = nc.dram_tensor("ones1", [1, 128], F32, kind="ExternalInput")
    gm_h = nc.dram_tensor("gammaB", [128, D], F32, kind="ExternalInput")
    bt_h = nc.dram_tensor("betaB", [128, D], F32, kind="ExternalInput")
    ib_h = nc.dram_tensor("identb", [128, 128], BF16, kind="ExternalInput")
    if_h = nc.dram_tensor("identf", [128, 128], F32, kind="ExternalInput")
    xo_h = nc.dram_tensor("xo", [TC, D], F32, kind="ExternalOutput")
    with ExitStack() as st:
        lnxT = st.enter_context(nc.sbuf_tensor("lnxT", [128, 16 * 512], BF16))
        inter = st.enter_context(nc.sbuf_tensor("inter", [128, 32 * 512], BF16))
        yd = st.enter_context(nc.sbuf_tensor("yd", [128, 16 * 512], F32))
        dnb = st.enter_context(nc.sbuf_tensor("dnb", [128, 4096], BF16))
        upb = st.enter_context(nc.sbuf_tensor("upb", [128, D], BF16))
        exb = st.enter_context(nc.sbuf_tensor("exb", [128, D], BF16))
        gwb = st.enter_context(nc.sbuf_tensor("gwb", [128, 128], BF16))
        eye8 = st.enter_context(nc.sbuf_tensor("eye8_s", [8, 8], F32))
        ones1 = st.enter_context(nc.sbuf_tensor("ones1_s", [1, 128], F32))
        gmb = st.enter_context(nc.sbuf_tensor("gmb", [128, D], F32))
        btb = st.enter_context(nc.sbuf_tensor("btb", [128, D], F32))
        fa = st.enter_context(nc.sbuf_tensor("fa", [128, D], F32))
        fb = st.enter_context(nc.sbuf_tensor("fb", [128, D], F32))
        ytok = st.enter_context(nc.sbuf_tensor("ytok", [128, D], F32))
        lnbf = st.enter_context(nc.sbuf_tensor("lnbf", [128, D], BF16))
        s1 = st.enter_context(nc.sbuf_tensor("s1", [128, 1], F32))
        s2 = st.enter_context(nc.sbuf_tensor("s2", [128, 1], F32))
        lgs = st.enter_context(nc.sbuf_tensor("lgs", [8, 512], F32))
        Rr = st.enter_context(nc.sbuf_tensor("Rr", [1, 8 * 512], F32))
        h1 = st.enter_context(nc.sbuf_tensor("h1", [1, 4 * 512], F32))
        l1 = st.enter_context(nc.sbuf_tensor("l1", [1, 4 * 512], F32))
        h2 = st.enter_context(nc.sbuf_tensor("h2", [1, 2 * 512], F32))
        l2 = st.enter_context(nc.sbuf_tensor("l2", [1, 2 * 512], F32))
        t1r = st.enter_context(nc.sbuf_tensor("t1r", [1, 512], F32))
        t2r = st.enter_context(nc.sbuf_tensor("t2r", [1, 512], F32))
        scr = st.enter_context(nc.sbuf_tensor("scr", [1, 512], F32))
        scB = st.enter_context(nc.sbuf_tensor("scB", [128, 512], F32))
        tmp5 = st.enter_context(nc.sbuf_tensor("tmp5", [128, 512], F32))
        ra = st.enter_context(nc.sbuf_tensor("ra", [128, 512], F32))
        identb = st.enter_context(nc.sbuf_tensor("identb_s", [128, 128], BF16))
        identf = st.enter_context(nc.sbuf_tensor("identf_s", [128, 128], F32))
        ps_u = st.enter_context(nc.psum_tensor("ps_u", [128, 512], F32))
        ps_d = st.enter_context(nc.psum_tensor("ps_d", [128, 512], F32))
        ps_e = st.enter_context(nc.psum_tensor("ps_e", [128, 512], F32))
        ps_l = st.enter_context(nc.psum_tensor("ps_l", [8, 512], F32))
        ps_r = st.enter_context(nc.psum_tensor("ps_r", [1, 512], F32))
        ps_b = st.enter_context(nc.psum_tensor("ps_b", [128, 512], F32))
        ptrb = st.enter_context(nc.psum_tensor("ptrb", [128, 128], BF16))
        ptrf = st.enter_context(nc.psum_tensor("ptrf", [128, 128], F32))
        sem = st.enter_context(nc.semaphore("gs"))
        sq = Seq(nc)
        sq.dma([(identb[:, :], ib_h[:, :]), (identf[:, :], if_h[:, :]),
                (eye8[:, :], e8_h[:, :]), (ones1[:, :], o2_h[:, :]),
                (gmb[:, :], gm_h[:, :]), (btb[:, :], bt_h[:, :]),
                (gwb[:, :], gw_h[:, :])])
        for tc in range(4):
            t0 = tc * 512
            _ln_emit(sq, nc, x_h, t0, 4, lnxT, 512, (fa, fb, lnbf, s1, s2), identb, ptrb)
            for half in range(2):
                for ntl in range(32):
                    nt = half * 32 + ntl
                    sq.dma([(upb[:, :], wu_h[nt, :, :])])
                    sq.mm([(ps_u[:, :], upb[:, kt * 128:(kt + 1) * 128],
                            lnxT[:, kt * 512:(kt + 1) * 512], kt == 0, kt == 15) for kt in range(16)])
                    sq.op("scalar", lambda e: e.activation(ra[:, :], ps_u[:, :], AF.Relu))
                    sq.op("scalar", lambda e, ntl=ntl: e.square(inter[:, ntl * 512:(ntl + 1) * 512], ra[:, :]))
                for ht in range(16):
                    sq.dma([(dnb[:, :], wd_h[ht, :, half * 4096:(half + 1) * 4096])])
                    sq.mm([(ps_d[:, :], dnb[:, ntl * 128:(ntl + 1) * 128],
                            inter[:, ntl * 512:(ntl + 1) * 512], ntl == 0, ntl == 31) for ntl in range(32)])
                    if half == 0:
                        sq.op("scalar", lambda e, ht=ht: e.copy(yd[:, ht * 512:(ht + 1) * 512], ps_d[:, :]))
                    else:
                        sq.op("vector", lambda e, ht=ht: e.tensor_add(yd[:, ht * 512:(ht + 1) * 512], yd[:, ht * 512:(ht + 1) * 512], ps_d[:, :]))
            sq.mm([(ps_l[:, :], gwb[:, kt * 8:(kt + 1) * 8],
                    lnxT[:, kt * 512:(kt + 1) * 512], kt == 0, kt == 15) for kt in range(16)])
            sq.op("scalar", lambda e: e.copy(lgs[:, :], ps_l[:, :]))
            for r in range(8):
                sq.mm([(ps_r[:, :], eye8[:, r:r + 1], lgs[:, :], True, True)])
                sq.op("scalar", lambda e, r=r: e.copy(Rr[:, r * 512:(r + 1) * 512], ps_r[:, :]))
            for i in range(4):
                sq.op("vector", lambda e, i=i: e.tensor_max(h1[:, i * 512:(i + 1) * 512], Rr[:, (2 * i) * 512:(2 * i + 1) * 512], Rr[:, (2 * i + 1) * 512:(2 * i + 2) * 512]))
                sq.op("vector", lambda e, i=i: e.tensor_tensor(l1[:, i * 512:(i + 1) * 512], Rr[:, (2 * i) * 512:(2 * i + 1) * 512], Rr[:, (2 * i + 1) * 512:(2 * i + 2) * 512], ALU.min))
            for i in range(2):
                sq.op("vector", lambda e, i=i: e.tensor_max(h2[:, i * 512:(i + 1) * 512], h1[:, (2 * i) * 512:(2 * i + 1) * 512], h1[:, (2 * i + 1) * 512:(2 * i + 2) * 512]))
                sq.op("vector", lambda e, i=i: e.tensor_tensor(t1r[:, :], h1[:, (2 * i) * 512:(2 * i + 1) * 512], h1[:, (2 * i + 1) * 512:(2 * i + 2) * 512], ALU.min))
                sq.op("vector", lambda e, i=i: e.tensor_max(t2r[:, :], l1[:, (2 * i) * 512:(2 * i + 1) * 512], l1[:, (2 * i + 1) * 512:(2 * i + 2) * 512]))
                sq.op("vector", lambda e, i=i: e.tensor_max(l2[:, i * 512:(i + 1) * 512], t1r[:, :], t2r[:, :]))
            sq.op("vector", lambda e: e.tensor_max(scr[:, :], h2[:, 0:512], h2[:, 512:1024]))
            sq.op("vector", lambda e: e.tensor_tensor(t1r[:, :], h2[:, 0:512], h2[:, 512:1024], ALU.min))
            sq.op("vector", lambda e: e.tensor_max(t2r[:, :], l2[:, 0:512], l2[:, 512:1024]))
            sq.op("vector", lambda e: e.tensor_max(t1r[:, :], t1r[:, :], t2r[:, :]))
            sq.op("vector", lambda e: e.tensor_add(scr[:, :], scr[:, :], t1r[:, :]))
            sq.mm([(ps_b[:, :], ones1[:, :], scr[:, :], True, True)])
            sq.op("scalar", lambda e: e.copy(scB[:, :], ps_b[:, :]))
            for dt in range(16):
                sq.dma([(exb[:, :], we_h[dt, :, :])])
                sq.mm([(ps_e[:, :], exb[:, kt * 128:(kt + 1) * 128],
                        lnxT[:, kt * 512:(kt + 1) * 512], kt == 0, kt == 15) for kt in range(16)])
                sq.op("vector", lambda e: e.tensor_mul(tmp5[:, :], ps_e[:, :], scB[:, :]))
                sq.op("vector", lambda e, dt=dt: e.tensor_add(yd[:, dt * 512:(dt + 1) * 512], yd[:, dt * 512:(dt + 1) * 512], tmp5[:, :]))
            for i in range(4):
                r0 = t0 + i * 128
                for dt in range(16):
                    sq.tr(ptrf[:, :], yd[:, dt * 512 + i * 128:dt * 512 + (i + 1) * 128], identf[:, :])
                    sq.op("scalar", lambda e, dt=dt: e.copy(ytok[:, dt * 128:(dt + 1) * 128], ptrf[:, :]))
                sq.dma([(fa[:, :], x_h[r0:r0 + 128, :])])
                sq.op("vector", lambda e: e.reduce_sum(s1[:, :], ytok[:, :], axis=AX.X))
                sq.op("scalar", lambda e: e.mul(s2[:, :], s1[:, :], -1.0 / D))
                sq.op("vector", lambda e: e.tensor_scalar(fb[:, :], ytok[:, :], s2[:, :], None, ALU.add))
                sq.op("scalar", lambda e: e.square(ytok[:, :], fb[:, :]))
                sq.op("vector", lambda e: e.reduce_sum(s1[:, :], ytok[:, :], axis=AX.X))
                sq.op("vector", lambda e: e.tensor_scalar(s2[:, :], s1[:, :], 1.0 / D, 1e-5, ALU.mult, ALU.add))
                sq.op("scalar", lambda e: e.sqrt(s1[:, :], s2[:, :]))
                sq.op("vector", lambda e: e.reciprocal(s2[:, :], s1[:, :]))
                sq.op("vector", lambda e: e.tensor_scalar(fb[:, :], fb[:, :], s2[:, :], None, ALU.mult))
                sq.op("vector", lambda e: e.tensor_mul(fb[:, :], fb[:, :], gmb[:, :]))
                sq.op("vector", lambda e: e.tensor_add(fb[:, :], fb[:, :], btb[:, :]))
                sq.op("vector", lambda e: e.tensor_add(fb[:, :], fb[:, :], fa[:, :]))
                sq.dma([(xo_h[r0:r0 + 128, :], fb[:, :])])
        with nc.Block() as block:
            sq.build(block, sem)
    return nc


def build_dequant():
    nc = bass.Bass("TRN2", target_bir_lowering=False)
    hs = {}
    for nm in ("g", "u", "d"):
        hs[nm + "p"] = nc.dram_tensor(nm + "p", [1024, 2048], I32, kind="ExternalInput")
        hs[nm + "s"] = nc.dram_tensor(nm + "s", [1024, 2048], F32, kind="ExternalInput")
        hs[nm + "e"] = nc.dram_tensor(nm + "e", [1024, 2048], BF16, kind="ExternalOutput")
        hs[nm + "o"] = nc.dram_tensor(nm + "o", [1024, 2048], BF16, kind="ExternalOutput")
    with ExitStack() as st:
        pi = st.enter_context(nc.sbuf_tensor("pi", [128, 2048], I32))
        pf = st.enter_context(nc.sbuf_tensor("pf", [128, 2048], F32))
        ssb = st.enter_context(nc.sbuf_tensor("ssb", [128, 2048], F32))
        hh = st.enter_context(nc.sbuf_tensor("hh", [128, 2048], F32))
        lo = st.enter_context(nc.sbuf_tensor("lo", [128, 2048], F32))
        a1 = st.enter_context(nc.sbuf_tensor("a1", [128, 2048], F32))
        a2 = st.enter_context(nc.sbuf_tensor("a2", [128, 2048], F32))
        a3 = st.enter_context(nc.sbuf_tensor("a3", [128, 2048], F32))
        a4 = st.enter_context(nc.sbuf_tensor("a4", [128, 2048], F32))
        ob = st.enter_context(nc.sbuf_tensor("ob", [128, 2048], BF16))
        sem = st.enter_context(nc.semaphore("gs"))
        sq = Seq(nc)

        def relu_b(dst, src, b):
            sq.op("vector", lambda e, b=b: e.tensor_scalar(a4[:, :], src[:, :], b, None, ALU.add))
            sq.op("scalar", lambda e: e.activation(dst[:, :], a4[:, :], AF.Relu))

        def decode(src, dst_h, r0):
            relu_b(a1, src, -7.0)
            relu_b(a2, src, -8.0)
            sq.op("vector", lambda e: e.tensor_sub(a1[:, :], a1[:, :], a2[:, :]))
            sq.op("vector", lambda e: e.tensor_scalar(a4[:, :], a1[:, :], 8.0, None, ALU.mult))
            sq.op("vector", lambda e: e.tensor_sub(a2[:, :], src[:, :], a4[:, :]))
            sq.op("vector", lambda e: e.tensor_scalar(a1[:, :], a1[:, :], -2.0, 1.0, ALU.mult, ALU.add))
            relu_b(a3, a2, -4.0)
            sq.op("vector", lambda e: e.tensor_add(a3[:, :], a3[:, :], a2[:, :]))
            sq.op("vector", lambda e: e.tensor_scalar(a3[:, :], a3[:, :], 0.5, None, ALU.mult))
            relu_b(a2, a2, -6.0)
            sq.op("vector", lambda e: e.tensor_add(a3[:, :], a3[:, :], a2[:, :]))
            sq.op("vector", lambda e: e.tensor_mul(a3[:, :], a3[:, :], a1[:, :]))
            sq.op("vector", lambda e: e.tensor_mul(ob[:, :], a3[:, :], ssb[:, :]))
            sq.dma([(dst_h[r0:r0 + 128, :], ob[:, :])])

        for nm in ("g", "u", "d"):
            for t in range(8):
                r0 = t * 128
                sq.dma([(pi[:, :], hs[nm + "p"][r0:r0 + 128, :]),
                        (ssb[:, :], hs[nm + "s"][r0:r0 + 128, :])])
                sq.op("vector", lambda e: e.tensor_copy(pf[:, :], pi[:, :]))
                sq.op("scalar", lambda e: e.activation(a1[:, :], pf[:, :], AF.Copy, bias=63.53125, scale=0.0625))
                sq.op("vector", lambda e: e.tensor_scalar(a2[:, :], a1[:, :], 8388608.0, -8388672.0, ALU.add, ALU.add))
                sq.op("vector", lambda e: e.tensor_copy(hh[:, :], a2[:, :]))
                sq.op("vector", lambda e: e.tensor_scalar(a2[:, :], hh[:, :], 16.0, None, ALU.mult))
                sq.op("vector", lambda e: e.tensor_sub(lo[:, :], pf[:, :], a2[:, :]))
                decode(lo, hs[nm + "e"], r0)
                decode(hh, hs[nm + "o"], r0)
        with nc.Block() as block:
            sq.build(block, sem)
    return nc


def build_prep(CH):
    nc = bass.Bass("TRN2", target_bir_lowering=False)
    w_h = nc.dram_tensor("w", [128, CH], F32, kind="ExternalInput")
    o_h = nc.dram_tensor("wb", [128, CH], BF16, kind="ExternalOutput")
    TW = 8192
    with ExitStack() as st:
        fi = st.enter_context(nc.sbuf_tensor("fi", [128, TW], F32))
        bo = st.enter_context(nc.sbuf_tensor("bo", [128, TW], BF16))
        sem = st.enter_context(nc.semaphore("gs"))
        sq = Seq(nc)
        for t in range(CH // TW):
            c0 = t * TW
            sq.dma([(fi[:, :], w_h[:, c0:c0 + TW])])
            sq.op("vector", lambda e: e.tensor_copy(bo[:, :], fi[:, :]))
            sq.dma([(o_h[:, c0:c0 + TW], bo[:, :])])
        with nc.Block() as block:
            sq.build(block, sem)
    return nc



_BUILD_CACHE = {}


def _get(name, fn, *a):
    key = (name,) + a
    if key not in _BUILD_CACHE:
        _BUILD_CACHE[key] = fn(*a)
    return _BUILD_CACHE[key]


def _run(nc, in_maps, outs):
    res = run_bass_kernel_spmd(nc, in_maps, core_ids=list(range(NCORE)))
    return [{k: np.asarray(r[k]) for k in outs} for r in res.results]


def _w5(W):
    K, N = W.shape
    return np.ascontiguousarray(
        W.reshape(K // 128, 128, N // 128, 128).transpose(2, 1, 0, 3).reshape(N // 128, 128, K))


def kernel(**inp):
    f32 = np.float32
    x = np.ascontiguousarray(inp["L0_attn_a_x"].astype(f32).reshape(T, D))

    j64 = np.arange(64, dtype=np.float64)
    invf = 1.0 / 10000.0 ** (2.0 * j64 / 128.0)
    pos = np.arange(S, dtype=np.float64)
    ang = pos[:, None] * invf[None, :]
    cosT = np.concatenate([np.cos(ang).T, np.cos(ang).T], 0).astype(f32)
    sinT = np.concatenate([np.sin(ang).T, np.sin(ang).T], 0).astype(f32)
    QS = f32(1.0 / np.sqrt(128.0))
    identb = np.eye(128, dtype=BFNP)
    identf = np.eye(128, dtype=f32)
    eye8 = np.eye(8, dtype=f32)
    onesT = np.ones((128, 1), dtype=f32)
    ones1 = np.ones((1, 128), dtype=f32)
    masks = np.zeros((4, 128, 512), dtype=f32)
    ii = np.arange(128)[:, None]
    jj = np.arange(512)[None, :]
    for r in range(4):
        masks[r] = np.where(jj <= 128 * r + ii, 0.0, -1e30)
    masks = masks.reshape(512, 512)
    lam = (2.0 ** (-8.0 * np.arange(1, 17) / 16.0)).astype(np.float64)
    r64 = np.arange(64, dtype=np.float64)
    diagT = np.zeros((16, 64, 64), dtype=f32)
    qdec = np.zeros((16, 64), dtype=np.float64)
    kdec = np.zeros((16, 64), dtype=np.float64)
    blk = np.exp(-lam * 64.0)
    for h in range(16):
        dif = r64[:, None] - r64[None, :]
        dg = np.where(dif >= 0, np.exp(-lam[h] * dif), 0.0)
        diagT[h] = dg.T.astype(f32)
        qdec[h] = np.exp(-lam[h] * (r64 + 1.0))
        kdec[h] = np.exp(-lam[h] * (63.0 - r64))
    perm = np.concatenate([np.concatenate([h * 128 + np.r_[0:128:2], h * 128 + np.r_[1:128:2]]) for h in range(16)])

    w5_list = {}
    w5_list["l0a_q"] = _w5(inp["L0_attn_a_wq"].astype(f32).T)
    w5_list["l0a_k"] = _w5(inp["L0_attn_a_wk"].astype(f32).T)
    w5_list["l0a_v"] = _w5(inp["L0_attn_a_wv"].astype(f32).T)
    w5_list["l0a_o"] = _w5(inp["L0_attn_a_wo"].astype(f32).T)
    w5_list["l0b_q"] = _w5(np.ascontiguousarray(inp["L0_attn_b_Wq"].astype(f32)[:, perm]))
    w5_list["l0b_k"] = _w5(np.ascontiguousarray(inp["L0_attn_b_Wk"].astype(f32)[:, perm]))
    w5_list["l0b_v"] = _w5(inp["L0_attn_b_Wv"].astype(f32))
    w5_list["l0b_o"] = _w5(inp["L0_attn_b_Wo"].astype(f32))
    wqkv = inp["L1_attn_a_weight_qkv"].astype(f32).T
    wq1 = np.concatenate([wqkv[:, g * 768:g * 768 + 512] for g in range(4)], 1)
    wk1 = np.concatenate([wqkv[:, g * 768 + 512:g * 768 + 640] for g in range(4)], 1)
    wv1 = np.concatenate([wqkv[:, g * 768 + 640:g * 768 + 768] for g in range(4)], 1)
    w5_list["l1a_q"] = _w5(wq1)
    w5_list["l1a_k"] = _w5(wk1)
    w5_list["l1a_v"] = _w5(wv1)
    w5_list["l1a_o"] = _w5(inp["L1_attn_a_weight_proj"].astype(f32).T)
    w5_list["l1b_q"] = _w5(np.ascontiguousarray(inp["L1_attn_b_Wq"].astype(f32).T[:, perm]))
    w5_list["l1b_k"] = _w5(np.ascontiguousarray(inp["L1_attn_b_Wk"].astype(f32).T[:, perm]))
    w5_list["l1b_v"] = _w5(inp["L1_attn_b_Wv"].astype(f32).T)
    w5_list["l1b_o"] = _w5(inp["L1_attn_b_Wo"].astype(f32).T)
    w5_list["l1f_u"] = _w5(inp["L1_ffn_up_w"].astype(f32))
    w5_list["l1f_d"] = _w5(inp["L1_ffn_down_w"].astype(f32))
    w5_list["l1f_e"] = _w5(inp["L1_ffn_expert_weight"].astype(f32))
    gwH = np.ascontiguousarray(
        inp["L1_ffn_gate_w"].astype(f32).reshape(16, 128, 8).transpose(1, 0, 2).reshape(128, 128))
    w5_list["l1f_g"] = gwH

    names = sorted(w5_list.keys())
    flats = [w5_list[n].ravel() for n in names]
    sizes = [f.size for f in flats]
    tot = sum(sizes)
    CHW = 8192 * 128
    padded = ((tot + NCORE * CHW - 1) // (NCORE * CHW)) * (NCORE * CHW)
    buf = np.zeros(padded, dtype=f32)
    buf[:tot] = np.concatenate(flats)
    CH = padded // (NCORE * 128)
    nc = _get("prep", build_prep, CH)
    shards = buf.reshape(NCORE, 128, CH)
    res = _run(nc, [{"w": np.ascontiguousarray(shards[c])} for c in range(NCORE)], ["wb"])
    bfbuf = np.concatenate([r["wb"].reshape(-1) for r in res])[:tot]
    wb = {}
    off = 0
    for n, s in zip(names, sizes):
        wb[n] = bfbuf[off:off + s].reshape(w5_list[n].shape)
        off += s

    gpT = np.ascontiguousarray(inp["L0_ffn_gate_W_packed"].astype(np.int32).T)
    gsT = np.ascontiguousarray(np.repeat(inp["L0_ffn_gate_scales"].astype(f32).T, 16, axis=0))
    upT = np.ascontiguousarray(inp["L0_ffn_up_W_packed"].astype(np.int32).T)
    usT = np.ascontiguousarray(np.repeat(inp["L0_ffn_up_scales"].astype(f32).T, 16, axis=0))
    dpT = np.ascontiguousarray(inp["L0_ffn_down_W_packed"].astype(np.int32).T)
    dsT = np.ascontiguousarray(np.repeat(inp["L0_ffn_down_scales"].astype(f32).T, 16, axis=0))
    nc = _get("dequant", build_dequant)
    maps = []
    for c in range(NCORE):
        cs = slice(c * 2048, (c + 1) * 2048)
        rs = slice(c * 1024, (c + 1) * 1024)
        maps.append({"gp": np.ascontiguousarray(gpT[:, cs]), "gs": np.ascontiguousarray(gsT[:, cs]),
                     "up": np.ascontiguousarray(upT[:, cs]), "us": np.ascontiguousarray(usT[:, cs]),
                     "dp": np.ascontiguousarray(dpT[rs]), "ds": np.ascontiguousarray(dsT[rs])})
    res = _run(nc, maps, ["ge", "go", "ue", "uo", "de", "do"])
    WgT = np.zeros((2048, 16384), dtype=BFNP)
    WuT = np.zeros((2048, 16384), dtype=BFNP)
    WdT = np.zeros((16384, 2048), dtype=BFNP)
    for c in range(NCORE):
        cs = slice(c * 2048, (c + 1) * 2048)
        WgT[0::2, cs] = res[c]["ge"]
        WgT[1::2, cs] = res[c]["go"]
        WuT[0::2, cs] = res[c]["ue"]
        WuT[1::2, cs] = res[c]["uo"]
        WdT[c * 2048:(c + 1) * 2048:2, :] = res[c]["de"]
        WdT[c * 2048 + 1:(c + 1) * 2048:2, :] = res[c]["do"]
    wb["l0f_g"] = _w5(WgT)
    wb["l0f_u"] = _w5(WuT)
    wb["l0f_d"] = _w5(WdT)

    def rope_tabs(scale):
        cq, sn = [], []
        for c in range(NCORE):
            s0 = (c % 2) * 2048
            cq.append(np.ascontiguousarray(cosT[:, s0:s0 + 2048]) * f32(scale))
            sn.append(np.ascontiguousarray(sinT[:, s0:s0 + 2048]) * f32(scale))
        return cq, sn
    cos1, sin1 = rope_tabs(1.0)
    cosq, sinq = rope_tabs(QS)

    def shard_tok(a):
        return [np.ascontiguousarray(a[c * TC:(c + 1) * TC]) for c in range(NCORE)]

    def qkv_launch(kind, wq, wk, wv, ctq, stq, ctk, stk):
        nc = _get("qkv" + kind, build_qkv, kind)
        maps = []
        for c in range(NCORE):
            m = {"x": xs[c], "wq": wb[wq], "wk": wb[wk], "wv": wb[wv], "identb": identb}
            if kind != "l1a":
                m.update({"cosq": ctq[c], "sinq": stq[c], "cosk": ctk[c], "sink": stk[c]})
            maps.append(m)
        res = _run(nc, maps, ["q", "k", "v"])
        qT = np.concatenate([r["q"] for r in res], axis=1)
        kT = np.concatenate([r["k"] for r in res], axis=1)
        vT = np.concatenate([r["v"] for r in res], axis=1)
        return qT, kT, vT

    def attn_pairs(qT, kT, vT, nkv):
        qs_, kts_, kns_, vns_ = [], [], [], []
        for c in range(NCORE):
            qq, kk, kn, vn = [], [], [], []
            for i in range(8):
                pi = c * 8 + i
                b, h = pi // 16, pi % 16
                g = h // (16 // nkv) if nkv < 16 else h
                qq.append(qT[h * 128:(h + 1) * 128, b * S:(b + 1) * S])
                kk.append(kT[g * 128:(g + 1) * 128, b * S:(b + 1) * S])
                kn.append(kT[g * 128:(g + 1) * 128, b * S:(b + 1) * S].T)
                vn.append(vT[g * 128:(g + 1) * 128, b * S:(b + 1) * S].T)
            qs_.append(np.ascontiguousarray(np.concatenate(qq, 0)))
            kts_.append(np.ascontiguousarray(np.concatenate(kk, 0)))
            kns_.append(np.ascontiguousarray(np.concatenate(kn, 0)))
            vns_.append(np.ascontiguousarray(np.concatenate(vn, 0)))
        return qs_, kts_, kns_, vns_

    def asm_o(res):
        oT = np.zeros((2048, T), dtype=BFNP)
        for c in range(NCORE):
            for i in range(8):
                pi = c * 8 + i
                b, h = pi // 16, pi % 16
                oT[h * 128:(h + 1) * 128, b * S:(b + 1) * S] = res[c]["o"][i * 128:(i + 1) * 128]
        return oT

    def wo_res_launch(oT, wname):
        nc = _get("wores", build_wo_res)
        maps = []
        for c in range(NCORE):
            maps.append({"a": np.ascontiguousarray(oT[:, c * TC:(c + 1) * TC]),
                         "w": wb[wname], "x": xs[c], "identf": identf})
        res = _run(nc, maps, ["xo"])
        return [r["xo"] for r in res]

    xs = shard_tok(x)

    qT, kT, vT = qkv_launch("l0a", "l0a_q", "l0a_k", "l0a_v", cos1, sin1, cos1, sin1)
    qs_, kts_, kns_, vns_ = attn_pairs(qT, kT, vT, 4)
    nc = _get("lightning", build_lightning)
    maps = []
    for c in range(NCORE):
        dg, qd, kd, bk = [], [], [], []
        for i in range(8):
            h = (c * 8 + i) % 16
            dg.append(diagT[h])
            qd.append(np.broadcast_to(qdec[h].astype(f32), (128, 64)))
            kd.append(kdec[h].astype(f32).reshape(64, 1))
            bk.append(np.full((128, 1), blk[h], dtype=f32))
        maps.append({"q": qs_[c], "kT": kts_[c], "kn": kns_[c], "vn": vns_[c],
                     "diagT": np.ascontiguousarray(np.concatenate(dg, 0)),
                     "qdecB": np.ascontiguousarray(np.concatenate(qd, 0)),
                     "kdec": np.ascontiguousarray(np.concatenate(kd, 0)),
                     "blkB": np.ascontiguousarray(np.concatenate(bk, 0))})
    res = _run(nc, maps, ["o"])
    oT = asm_o(res)
    xs = wo_res_launch(oT, "l0a_o")

    def l0ffn_launch():
        nc = _get("l0ffn", build_l0ffn)
        maps = [{"x": xs[c], "wg": wb["l0f_g"], "wu": wb["l0f_u"], "wd": wb["l0f_d"],
                 "identb": identb, "identf": identf} for c in range(NCORE)]
        res = _run(nc, maps, ["xo"])
        return [r["xo"] for r in res]

    xs = l0ffn_launch()

    def mha_launch(pre):
        qT, kT, vT = qkv_launch("mha", pre + "_q", pre + "_k", pre + "_v", cosq, sinq, cos1, sin1)
        qs_, kts_, _, vns_ = attn_pairs(qT, kT, vT, 16)
        nc = _get("flash", build_flash)
        maps = [{"q": qs_[c], "kT": kts_[c], "vn": vns_[c], "masks": masks, "identb": identb}
                for c in range(NCORE)]
        res = _run(nc, maps, ["o"])
        return asm_o(res)

    oT = mha_launch("l0b")
    xs = wo_res_launch(oT, "l0b_o")
    xs = l0ffn_launch()

    qT, kT, vT = qkv_launch("l1a", "l1a_q", "l1a_k", "l1a_v", None, None, None, None)
    nc = _get("smallattn", build_smallattn)
    maps = []
    for c in range(NCORE):
        def msl(a):
            return np.ascontiguousarray(np.concatenate(
                [a[:, s * S + c * 512:s * S + (c + 1) * 512] for s in range(4)], 1))
        maps.append({"q": msl(qT), "k": msl(kT), "v": msl(vT), "onesT": onesT, "ones1": ones1})
    res = _run(nc, maps, ["o"])
    oT = np.zeros((2048, T), dtype=BFNP)
    for c in range(NCORE):
        for s in range(4):
            oT[:, s * S + c * 512:s * S + (c + 1) * 512] = res[c]["o"][:, s * 512:(s + 1) * 512]
    xs = wo_res_launch(oT, "l1a_o")

    def l1ffn_launch():
        gmB = np.ascontiguousarray(np.broadcast_to(inp["L1_ffn_ln_gamma"].astype(f32), (128, D)))
        btB = np.ascontiguousarray(np.broadcast_to(inp["L1_ffn_ln_beta"].astype(f32), (128, D)))
        nc = _get("l1ffn", build_l1ffn)
        maps = [{"x": xs[c], "wu": wb["l1f_u"], "wd": wb["l1f_d"], "we": wb["l1f_e"],
                 "gw": wb["l1f_g"], "eye8": eye8, "ones1": ones1, "gammaB": gmB,
                 "betaB": btB, "identb": identb, "identf": identf} for c in range(NCORE)]
        res = _run(nc, maps, ["xo"])
        return [r["xo"] for r in res]

    xs = l1ffn_launch()
    oT = mha_launch("l1b")
    xs = wo_res_launch(oT, "l1b_o")
    xs = l1ffn_launch()

    out = np.concatenate(xs, 0).reshape(NB, S, D).astype(np.float32)
    return out
